# version 01 (final)
# baseline (speedup 1.0000x reference)
import numpy as np
import ml_dtypes
import contextlib
import concourse.bass as bass
import concourse.mybir as mybir
from concourse.bass_utils import run_bass_kernel_spmd

F32 = mybir.dt.float32
BF16 = mybir.dt.bfloat16
AF = mybir.ActivationFunctionType
ALU = mybir.AluOpType
AX = mybir.AxisListType.X
BFnp = ml_dtypes.bfloat16

INTER, CONV_DIM, TOK, NCH = 8192, 10240, 1024, 8
EPS = 1e-5


class Bld:
    def __init__(self, nc, es, block):
        self.nc, self.es, self.block = nc, es, block
        self.sems, self.cnt = {}, {}

    def sem(self, *names):
        for name in names:
            self.sems[name] = self.es.enter_context(self.nc.semaphore(name))
            self.cnt[name] = 0

    def inc(self, name, instr, k=1):
        instr.then_inc(self.sems[name], k)
        self.cnt[name] += k
        return self.cnt[name]

    def w(self, eng, name, val=None):
        eng.wait_ge(self.sems[name], self.cnt[name] if val is None else val)

    def sb(self, name, shape, dtype):
        return self.es.enter_context(self.nc.sbuf_tensor(name, shape, dtype))

    def ps(self, name, shape, dtype=F32):
        return self.es.enter_context(self.nc.psum_tensor(name, shape, dtype))


def build(PH=99, dev_outputs=(), N2B=NCH):
    nc = bass.Bass("TRN2", target_bir_lowering=False)
    T = {}

    def din(name, shape, dtype=F32):
        T[name] = nc.dram_tensor(name, shape, dtype, kind="ExternalInput").ap()

    def dsc(name, shape, dtype=F32, **kw):
        T[name] = nc.dram_tensor(name, shape, dtype, **kw).ap()

    din("xT", [4096, TOK]); din("W_xbc", [80, 32, 128, 128]); din("W_gate", [64, 32, 128, 128])
    din("W_dt", [32, 128, 128]); din("W2", [32, 64, 128, 128])
    din("convw", [128, 320]); din("convb", [128, 80]); din("dt_bias", [128, 1])
    din("D_row", [1, 128]); din("norm_w", [128, 64])
    din("router_wT", [32, 128, 8]); din("router_b", [8, 1])
    din("AT", [32, 128, 128]); din("BT", [128, 4096])
    din("mask8e", [128, 8]); din("hmaske", [128, 8]); din("fsele", [128, 64])

    dsc("sp_xbc", [CONV_DIM, TOK], BF16)
    dsc("sp_gate", [NCH, 64, 128, 128], BF16)
    dsc("sp_conv", [NCH, 80, 128, 128], BF16)
    dsc("sp_ctm", [NCH, 72, 128, 128], BF16)
    dsc("sp_ygn", [64, NCH, 128, 128], BF16)
    dsc("sp_r2", [32, 128, TOK])
    dsc("sp_x2f", [32, 128, TOK])
    dsc("sp_x2b", [32, 128, TOK], BF16)
    dsc("bnc", [16, 16384])
    dsc("bnc_cs3", [NCH, 3, 16384], BF16)
    dsc("sp_y", [NCH, 128, INTER], BF16)
    dsc("ag1_s", [128, 240]); dsc("ag1_r", [1024, 240], F32, addr_space="Shared")
    dsc("ag2_s", [128, 8320]); dsc("ag2_r", [1024, 8320], F32, addr_space="Shared")

    T["out"] = nc.dram_tensor("out", [TOK, 4096], F32, kind="ExternalOutput").ap()
    DEV = {}
    for name, shape in [("d_xbc", [CONV_DIM, TOK]), ("d_gate", [INTER, TOK]),
                        ("d_dtraw", [128, TOK]), ("d_conv", [CONV_DIM, TOK]),
                        ("d_ctm", [NCH, 72 * 128, 128]), ("d_dtv", [128, TOK]),
                        ("d_cs", [128, TOK]), ("d_srun", [128, INTER]),
                        ("d_carry", [128, INTER]), ("d_y", [NCH * 128, INTER]),
                        ("d_ygn", [INTER, TOK]), ("d_x2", [4096, TOK]),
                        ("d_w", [TOK, 8]), ("d_hw", [128, TOK]),
                        ("d_halomy", [128, 240]), ("d_hsend", [128, 240]),
                        ("d_hrecv", [128, 1920])]:
        if name in dev_outputs:
            T[name] = nc.dram_tensor(name, shape, F32, kind="ExternalOutput").ap()
            DEV[name] = T[name]

    C = {
        "identbf": np.eye(128, dtype=BFnp),
        "identf": np.eye(128, dtype=np.float32),
        "ones128f": np.ones((128, 128), np.float32),
        "ones1f": np.ones((1, 128), np.float32),
        "ones3bf": np.ones((3, 128), BFnp),
        "trilbf": (np.arange(128)[None, :] >= np.arange(128)[:, None]).astype(BFnp),
        "acol": -np.arange(1, 129, dtype=np.float32).reshape(128, 1),
        "egsel": np.concatenate([np.eye(8, dtype=BFnp)[g:g + 1].repeat(128, 0) for g in range(8)], 1),
        "rsel": np.repeat(np.eye(8, dtype=BFnp), 16, axis=1),
    }
    Cd = {k: nc.inline_tensor(np.ascontiguousarray(v), "c_" + k).ap() for k, v in C.items()}

    es = contextlib.ExitStack()
    with es:
        block = es.enter_context(nc.Block())
        b = Bld(nc, es, block)

        cs = {}
        for k in ["identbf", "ones3bf", "trilbf", "egsel", "rsel"]:
            cs[k] = b.sb("cs_" + k, list(C[k].shape), BF16)
        for k in ["identf", "ones128f", "ones1f", "acol"]:
            cs[k] = b.sb("cs_" + k, list(C[k].shape), F32)
        for k, shape in [("dt_bias", [128, 1]), ("convw", [128, 320]), ("convb", [128, 80]),
                         ("norm_w", [128, 64]), ("mask8e", [128, 8]), ("hmaske", [128, 8]),
                         ("fsele", [128, 64]), ("router_b", [8, 1])]:
            cs[k] = b.sb("cs_" + k, shape, F32)
        cs["one"] = b.sb("cs_one", [128, 1], F32)
        cs["eps8"] = b.sb("cs_eps8", [8, 1], F32)
        cs["Dbc"] = b.sb("cs_Dbc", [128, 128], F32)
        cs["Drow"] = b.sb("cs_Drow", [1, 128], F32)

        psA = b.ps("psA", [128, 1024], F32)
        psB = b.ps("psB", [128, 1024], F32)
        psE = b.ps("psE", [128, 2048], F32)

        b.sem("s_const", "s_xf0", "s_xf1", "s_gx", "s_gd", "s_pe", "s_dv", "s_ac", "s_po",
              "s_gw0", "s_gw1", "s_gw2", "s_cc", "s_hrecv",
              "s_cvin0", "s_cvin1", "s_spcv0", "s_spcv1", "s_spctm", "s_dev", "s_chx",
              "s_sp1_0", "s_sp1_1", "s_sp1_2", "s_sp1_3")

        p_dtraw = b.sb("p_dtraw", [128, TOK], F32)

        esA = contextlib.ExitStack()
        bA = Bld(nc, esA, block)
        with esA:
            x_bf = bA.sb("x_bf", [128, 32 * TOK], BF16)
            xf = [bA.sb(f"xf{i}", [128, TOK], F32) for i in range(2)]
            wdt_sb = bA.sb("wdt_sb", [128, 32 * 128], F32)
            wsl = [bA.sb(f"wsl{i}", [128, 32 * 128], BF16) for i in range(3)]
            ev = [bA.sb(f"ev{i}", [128, TOK], BF16) for i in range(4)]
            halosend = bA.sb("halosend", [128, 240], F32)
            halo_my = bA.sb("halo_my", [128, 240], F32)
            halo_pp = bA.sb("halo_pp", [128, 240], F32)
            cv = [bA.sb(f"cv{i}", [128, 1024], BF16) for i in range(2)]
            xc = [bA.sb(f"xc{i}", [128, 1027], BF16) for i in range(2)]
            diag = [bA.sb(f"diag{i}", [128, 4 * 128], BF16) for i in range(2)]
            tmst = bA.sb("tmst", [128, 8 * 128], BF16)
            hrecv = bA.sb("hrecv", [128, 8 * 240], F32)

            p1_tiles = [("x", ft) for ft in range(80)] + [("g", ft) for ft in range(64)]
            NW = 3
            NCONST = 18
            PE_P0 = 32
            PE_P1 = PE_P0 + 144
            pe_convmm, pe_convtr = {}, {}
            c_ = PE_P1
            for ct in range(80):
                c_ += 1; pe_convmm[ct] = c_
                if ct < 72:
                    c_ += 1; pe_convtr[ct] = c_
            DV_P1 = 1 + 80 + 1
            dv_diag, dv_tmev = {}, {}
            c_ = DV_P1
            for it in range(82):
                if 0 <= it - 2 < 72:
                    c_ += 1; dv_tmev[it - 2] = c_
                if it < 80:
                    c_ += 1; dv_diag[it] = c_
            AC_P1 = 64

            @block.tensor
            def _(t):
                b.w(t, "s_gd", 16)
                for k in range(32):
                    b.w(t, f"s_xf{k % 2}", 16 * (k // 2 + 1))
                    for cg in range(2):
                        ins = t.matmul(psA[:, cg * 512:(cg + 1) * 512],
                                       wdt_sb[:, k * 128:(k + 1) * 128],
                                       xf[k % 2][:, cg * 512:(cg + 1) * 512],
                                       start=(k == 0), stop=(k == 31))
                    b.inc("s_pe", ins)
                b.w(t, "s_gx", 16)
                b.w(t, "s_dv", 1)
                for i, _kf in enumerate(p1_tiles):
                    b.w(t, f"s_gw{i % NW}", 16 * (i // NW + 1))
                    if i >= 2:
                        j = i - 2
                        if j < 80:
                            b.w(t, "s_dv", 1 + (j + 1))
                        else:
                            b.w(t, "s_ac", (j - 80) + 1)
                    ps_ = psA if i % 2 == 0 else psB
                    for cg in range(2):
                        for k in range(32):
                            ins = t.matmul(ps_[:, cg * 512:(cg + 1) * 512],
                                           wsl[i % NW][:, k * 128:(k + 1) * 128],
                                           x_bf[:, k * TOK + cg * 512:k * TOK + cg * 512 + 512],
                                           start=(k == 0), stop=(k == 31))
                    b.inc("s_pe", ins)
                for ct in range(80):
                    b.w(t, "s_dv", dv_diag[ct])
                    if ct >= 2:
                        b.w(t, "s_ac", AC_P1 + (ct - 2) + 1)
                    ps_ = psA if ct % 2 == 0 else psB
                    for k in range(4):
                        for cg in range(2):
                            ins = t.matmul(ps_[:, cg * 512:(cg + 1) * 512],
                                           diag[ct % 2][:, k * 128:(k + 1) * 128],
                                           xc[ct % 2][:, k + cg * 512:k + cg * 512 + 512],
                                           start=(k == 0), stop=(k == 3))
                    b.inc("s_pe", ins)
                    assert b.cnt["s_pe"] == pe_convmm[ct]
                    if ct < 72:
                        b.w(t, "s_ac", AC_P1 + ct + 1)
                        if ct >= 1:
                            b.w(t, "s_dv", dv_tmev[ct - 1])
                        psCbf = psE[:, 0:512].bitcast(BF16)
                        for j in range(8):
                            ins = t.transpose(psCbf[:, j * 128:(j + 1) * 128],
                                              cv[ct % 2][:, j * 128:(j + 1) * 128],
                                              cs["identbf"][:])
                        b.inc("s_pe", ins)
                        assert b.cnt["s_pe"] == pe_convtr[ct]

            @block.vector
            def _(v):
                b.w(v, "s_pe", PE_P0)
                b.inc("s_dv", v.tensor_copy(p_dtraw[:], psA[:]))
                for i in range(80):
                    b.w(v, "s_pe", PE_P0 + i + 1)
                    if i >= 4:
                        b.w(v, f"s_sp1_{i % 4}", 16 * (i // 4))
                    ps_ = psA if i % 2 == 0 else psB
                    v.tensor_copy(ev[i % 4][:], ps_[:])
                    ins = v.tensor_copy(halosend[:, i * 3:(i + 1) * 3], ps_[:, 1021:1024])
                    b.inc("s_dv", ins)
                b.w(v, "s_hrecv", 16 * 8)
                b.w(v, "s_const", 16 * NCONST)
                accs = [halo_pp, halo_my]
                b.inc("s_chx", v.tensor_scalar(accs[0][:], hrecv[:, 0:240], cs["hmaske"][:, 0:1], None, ALU.mult))
                for j in range(1, 8):
                    src_, dst_ = accs[(j + 1) % 2], accs[j % 2]
                    b.w(v, "s_chx")
                    ins = v.scalar_tensor_tensor(dst_[:], hrecv[:, j * 240:(j + 1) * 240],
                                                 cs["hmaske"][:, j:j + 1], src_[:], ALU.mult, ALU.add)
                    if j < 7:
                        b.inc("s_chx", ins)
                b.inc("s_dv", ins)
                assert b.cnt["s_dv"] == DV_P1
                b.w(v, "s_dv", DV_P1)
                for it in range(82):
                    if 0 <= it - 2 < 72:
                        ct = it - 2
                        b.w(v, "s_pe", pe_convtr[ct])
                        if ct >= 1:
                            b.w(v, "s_spctm", 16 * ct)
                        ins = v.tensor_copy(tmst[:], psE[:, 0:512].bitcast(BF16)[:, 0:1024])
                        b.inc("s_dv", ins)
                        assert b.cnt["s_dv"] == dv_tmev[ct]
                    if it < 80:
                        ct = it
                        b.w(v, f"s_cvin{ct % 2}", 16 * (ct // 2 + 1))
                        if ct >= 2:
                            b.w(v, "s_ac", AC_P1 + (ct - 2) + 1)
                        for k in range(4):
                            v.tensor_scalar(diag[ct % 2][:, k * 128:(k + 1) * 128], cs["identbf"][:],
                                            cs["convw"][:, ct * 4 + k:ct * 4 + k + 1], None, ALU.mult)
                        ins = v.tensor_copy(xc[ct % 2][:, 0:3], halo_my[:, ct * 3:(ct + 1) * 3])
                        b.inc("s_dv", ins)
                        assert b.cnt["s_dv"] == dv_diag[ct]

            @block.scalar
            def _(s):
                for i in range(80, 144):
                    b.w(s, "s_pe", PE_P0 + i + 1)
                    b.w(s, f"s_sp1_{i % 4}", 16 * (i // 4))
                    ps_ = psA if i % 2 == 0 else psB
                    b.inc("s_ac", s.activation(ev[i % 4][:], ps_[:], AF.Silu))
                for ct in range(80):
                    b.w(s, "s_pe", pe_convmm[ct])
                    if ct >= 2:
                        b.w(s, f"s_spcv{ct % 2}", 16 * (ct // 2))
                    ps_ = psA if ct % 2 == 0 else psB
                    b.inc("s_ac", s.activation(cv[ct % 2][:], ps_[:], AF.Silu,
                                               bias=cs["convb"][:, ct:ct + 1], scale=1.0))

            @block.gpsimd
            def _(g):
                b.inc("s_gx", g.dma_start(out=x_bf[:].rearrange("p (k t) -> p k t", k=32), in_=T["xT"].rearrange("(k p) t -> p k t", p=128)), 16)
                b.inc("s_gd", g.dma_start(out=wdt_sb[:].rearrange("p (a c) -> p a c", a=32), in_=T["W_dt"].rearrange("a p c -> p a c")), 16)
                for i, (kind, ft) in enumerate(p1_tiles):
                    src = (T["W_xbc"] if kind == "x" else T["W_gate"])[ft].rearrange("a p c -> p a c")
                    if i >= NW:
                        b.w(g, "s_pe", PE_P0 + (i - NW) + 1)
                    b.inc(f"s_gw{i % NW}", g.dma_start(out=wsl[i % NW][:].rearrange("p (a c) -> p a c", a=32), in_=src), 16)
                b.w(g, "s_dv", 1 + 80)
                b.inc("s_gd", g.dma_start(out=T["ag1_s"], in_=halosend[:]), 16)
                b.w(g, "s_gd", 32)
                b.inc("s_cc", g.collective_compute(
                    "AllGather", ALU.bypass, replica_groups=[list(range(8))],
                    ins=[T["ag1_s"]], outs=[T["ag1_r"]]))

            @block.sync
            def _(sync):
                for key in ["identbf", "identf", "ones128f", "ones1f", "ones3bf",
                            "trilbf", "acol", "egsel", "rsel"]:
                    b.inc("s_const", sync.dma_start(out=cs[key][:], in_=Cd[key]), 16)
                for key in ["dt_bias", "convw", "convb", "norm_w", "mask8e", "hmaske",
                            "fsele", "router_b"]:
                    b.inc("s_const", sync.dma_start(out=cs[key][:], in_=T[key]), 16)
                b.inc("s_const", sync.dma_start(out=cs["Drow"][:], in_=T["D_row"]), 16)
                assert b.cnt["s_const"] == 16 * NCONST
                for k in range(32):
                    if k >= 2:
                        b.w(sync, "s_pe", k - 1)
                    b.inc(f"s_xf{k % 2}", sync.dma_start(out=xf[k % 2][:], in_=T["xT"][k * 128:(k + 1) * 128, :]), 16)
                for i, (kind, ft) in enumerate(p1_tiles):
                    if kind == "x":
                        b.w(sync, "s_dv", 1 + (i + 1))
                        dst = T["sp_xbc"][ft * 128:(ft + 1) * 128, :]
                    else:
                        b.w(sync, "s_ac", (i - 80) + 1)
                        dst = T["sp_gate"][:, ft, :, :].rearrange("c p t -> p c t")
                    b.inc(f"s_sp1_{i % 4}", sync.dma_start(
                        out=dst, in_=ev[i % 4][:].rearrange("p (c t) -> p c t", c=8)), 16)
                b.w(sync, "s_cc", 1)
                for j in range(8):
                    b.inc("s_hrecv", sync.dma_start(out=hrecv[:, j * 240:(j + 1) * 240],
                                                    in_=T["ag1_r"][j * 128:(j + 1) * 128, :]), 16)
                for it in range(82):
                    if it < 80:
                        ct = it
                        for sl in range(4):
                            b.w(sync, f"s_sp1_{sl}", 16 * 36)
                        if ct >= 2:
                            b.w(sync, "s_ac", AC_P1 + (ct - 2) + 1)
                        b.inc(f"s_cvin{ct % 2}", sync.dma_start(out=xc[ct % 2][:, 3:1027],
                                                       in_=T["sp_xbc"][ct * 128:(ct + 1) * 128, :]), 16)
                    if it >= 2:
                        ct = it - 2
                        b.w(sync, "s_ac", AC_P1 + ct + 1)
                        dst = T["sp_conv"][:, ct, :, :].rearrange("c p t -> p c t")
                        b.inc(f"s_spcv{ct % 2}", sync.dma_start(
                            out=dst, in_=cv[ct % 2][:].rearrange("p (c t) -> p c t", c=8)), 16)
                        if ct < 72:
                            b.w(sync, "s_dv", dv_tmev[ct])
                            dst = T["sp_ctm"][:, ct, :, :].rearrange("c t f -> t c f")
                            b.inc("s_spctm", sync.dma_start(
                                out=dst, in_=tmst[:].rearrange("p (c f) -> p c f", c=8)), 16)

            if PH <= 15:
                @block.gpsimd
                def _(sync):
                    b.w(sync, "s_spcv0"); b.w(sync, "s_spcv1")
                    b.w(sync, "s_spctm")
                    for sl in range(4):
                        b.w(sync, f"s_sp1_{sl}")
                    b.w(sync, "s_dv")
                    if "d_xbc" in DEV:
                        b.inc("s_dev", sync.dma_start(out=DEV["d_xbc"], in_=T["sp_xbc"]), 16)
                    if "d_gate" in DEV:
                        for f in range(64):
                            b.inc("s_dev", sync.dma_start(
                                out=DEV["d_gate"][f * 128:(f + 1) * 128, :].rearrange("p (c t) -> p c t", c=8),
                                in_=T["sp_gate"][:, f, :, :].rearrange("c p t -> p c t")), 16)
                    if "d_dtraw" in DEV:
                        b.inc("s_dev", sync.dma_start(out=DEV["d_dtraw"], in_=p_dtraw[:]), 16)
                    if "d_conv" in DEV:
                        for f in range(80):
                            b.inc("s_dev", sync.dma_start(
                                out=DEV["d_conv"][f * 128:(f + 1) * 128, :].rearrange("p (c t) -> p c t", c=8),
                                in_=T["sp_conv"][:, f, :, :].rearrange("c p t -> p c t")), 16)
                    if "d_halomy" in DEV:
                        b.inc("s_dev", sync.dma_start(out=DEV["d_halomy"], in_=halo_my[:]), 16)
                    if "d_hsend" in DEV:
                        b.inc("s_dev", sync.dma_start(out=DEV["d_hsend"], in_=halosend[:]), 16)
                    if "d_hrecv" in DEV:
                        b.inc("s_dev", sync.dma_start(out=DEV["d_hrecv"], in_=hrecv[:]), 16)
                    if "d_ctm" in DEV:
                        for f in range(72):
                            b.inc("s_dev", sync.dma_start(
                                out=DEV["d_ctm"][:, f * 128:(f + 1) * 128, :].rearrange("c p t -> p c t"),
                                in_=T["sp_ctm"][:, f, :, :].rearrange("c p t -> p c t")), 16)
                    b.w(sync, "s_dev")
                return nc, T

        A_BAR = {s_: b.cnt[s_] for s_ in ["s_pe", "s_dv", "s_ac", "s_spcv0", "s_spcv1",
                                           "s_spctm", "s_sp1_0", "s_sp1_1", "s_sp1_2", "s_sp1_3"]}
        b.sem("s_ctr", "s_ctrb", "s_cc2", "s_spyg", "s_gt", "s_agr",
              "s_b0", "s_b1", "s_b2", "s_b3", "s_b4", "s_ag2s",
              "s_cs3r0", "s_cs3r1", "s_srj0", "s_srj1")
        esB = contextlib.ExitStack()
        bB = Bld(nc, esB, block)
        with esB:
            p_carry = bB.sb("p_carry", [128, INTER], F32)
            p_carrybf = bB.sb("p_carrybf", [128, INTER], BF16)
            p_dtvfm = bB.sb("p_dtvfm", [128, TOK], F32)
            p_csfm = bB.sb("p_csfm", [128, TOK], F32)
            p_dtvtm = bB.sb("p_dtvtm", [128, TOK], F32)
            p_ddttm = bB.sb("p_ddttm", [128, TOK], F32)
            p_cstm = bB.sb("p_cstm", [128, TOK], F32)
            p_ecstm = bB.sb("p_ecstm", [128, TOK], F32)
            p_cdbc = bB.sb("p_cdbc", [128, TOK], F32)
            p_cdprod = bB.sb("p_cdprod", [128, 128], F32)
            xtm = bB.sb("xtm", [128, 72 * 128], BF16)
            xdg = bB.sb("xdg", [128, 1024], BF16)
            xdt_sb = bB.sb("xdt_sb", [128, 1024], BF16)
            tmp0 = bB.sb("tmp0", [128, 128], F32)
            tmp1 = bB.sb("tmp1", [128, 128], F32)
            sp_sb = bB.sb("sp_sb", [128, 128], F32)
            sp2_sb = bB.sb("sp2_sb", [128, 128], F32)
            dec_sb = bB.sb("dec_sb", [128, 128], F32)
            ddt_sb = bB.sb("ddt_sb", [128, 128], F32)
            cs3h = bB.sb("cs3h", [128, 128], BF16)
            tmpl = bB.sb("tmpl", [128, 128], F32)
            lob = bB.sb("lob", [128, 128], BF16)
            lo2 = bB.sb("lo2", [128, 128], BF16)
            cdrow = bB.sb("cdrow", [1, 128], F32)
            cdrow_e = bB.sb("cdrow_e", [1, 128], F32)
            tmpc = bB.sb("tmpc", [128, 1024], F32)

            for eng_name in ["tensor", "vector", "scalar", "sync", "gpsimd"]:
                def _bar(e):
                    for s_, v_ in A_BAR.items():
                        e.wait_ge(b.sems[s_], v_)
                getattr(block, eng_name)(_bar)

            @block.vector
            def _(v):
                b.inc("s_dv", v.memset(p_carry[:], 0.0))
                b.inc("s_dv", v.memset(p_cdprod[:], 1.0))
                b.inc("s_dv", v.memset(cs["one"][:], 1.0))
                b.inc("s_dv", v.memset(cs["eps8"][:], EPS))
            DV_INIT = b.cnt["s_dv"]

            @block.tensor
            def _(t):
                b.w(t, "s_const", 16 * 18)
                b.inc("s_pe", t.matmul(psE[:, 0:128], cs["ones1f"][:], cs["Drow"][:],
                                       start=True, stop=True))
            PE_DBC = b.cnt["s_pe"]

            @block.vector
            def _(v):
                b.w(v, "s_pe", PE_DBC)
                b.inc("s_dv", v.tensor_copy(cs["Dbc"][:], psE[:, 0:128]))
            DV_DBC = b.cnt["s_dv"]

            DVC = 14 + 24
            ACC = 5
            PEC = 10
            for c in range(NCH):
                csl = slice(c * 128, (c + 1) * 128)
                AC0 = b.cnt["s_ac"]; PE0 = b.cnt["s_pe"]; DV0 = b.cnt["s_dv"]

                @block.scalar
                def _(s):
                    b.w(s, "s_dv", DV0 + 1)
                    b.inc("s_ac", s.activation(sp_sb[:], tmp0[:], AF.Exp))
                    b.w(s, "s_ac")
                    b.inc("s_ac", s.activation(sp2_sb[:], sp_sb[:], AF.Ln, bias=cs["one"][:], scale=1.0))
                    b.w(s, "s_dv", DV0 + 4)
                    b.inc("s_ac", s.activation(dec_sb[:], p_csfm[:, csl], AF.Exp,
                                               bias=p_csfm[:, c * 128 + 127:c * 128 + 128], scale=-1.0))
                    b.w(s, "s_dv", DV0 + 6)
                    b.inc("s_ac", s.activation(p_ecstm[:, csl], p_cstm[:, csl], AF.Exp))
                    b.w(s, "s_pe", PE0)
                    b.w(s, "s_b4", 16 * (c + 1))
                    b.inc("s_ac", s.activation(cdrow_e[:], cdrow[:], AF.Exp))

                @block.vector
                def _(v):
                    b.inc("s_dv", v.tensor_scalar(tmp0[:], p_dtraw[:, csl], cs["dt_bias"][:], None, ALU.add))
                    b.w(v, "s_ac", AC0 + 2)
                    b.inc("s_dv", v.tensor_scalar(p_dtvfm[:, csl], sp2_sb[:], 100.0, 1e-3, ALU.min, ALU.max))
                    b.w(v, "s_dv")
                    b.inc("s_dv", v.tensor_scalar(tmp1[:], p_dtvfm[:, csl], cs["acol"][:], None, ALU.mult))
                    b.w(v, "s_dv")
                    b.inc("s_dv", v.tensor_tensor_scan(p_csfm[:, csl], cs["ones128f"][:], tmp1[:],
                                                       0.0, ALU.mult, ALU.add))
                    b.w(v, "s_ac", AC0 + 3)
                    b.inc("s_dv", v.tensor_tensor(ddt_sb[:], p_dtvfm[:, csl], dec_sb[:], ALU.mult))
                    b.w(v, "s_pe", PE0 + 1)
                    b.inc("s_dv", v.tensor_copy(p_cstm[:, csl], psE[:, 0:128]))
                    b.inc("s_dv", v.tensor_copy(p_dtvtm[:, csl], psE[:, 128:256]))
                    b.inc("s_dv", v.tensor_copy(p_ddttm[:, csl], psE[:, 256:384]))
                    b.w(v, "s_b0", 16 * c)
                    b.inc("s_dv", v.tensor_copy(cs3h[:], p_csfm[:, csl]))
                    b.w(v, "s_dv")
                    b.inc("s_dv", v.tensor_tensor(tmpl[:], p_csfm[:, csl], cs3h[:], ALU.subtract))
                    b.w(v, "s_dv")
                    b.w(v, "s_b1", 16 * c)
                    b.inc("s_dv", v.tensor_copy(lob[:], tmpl[:]))
                    b.w(v, "s_dv")
                    b.w(v, "s_b2", 16 * c)
                    b.inc("s_dv", v.tensor_tensor(lo2[:], tmpl[:], lob[:], ALU.subtract))
                    b.w(v, "s_pe", PE0 + 2)
                    b.inc("s_dv", v.tensor_copy(p_cdbc[:, csl], psE[:, 1536:1664]))
                    b.w(v, "s_dv")
                    b.inc("s_dv", v.tensor_tensor(p_cdprod[:], p_cdprod[:], p_cdbc[:, csl], ALU.mult))
                    for g in range(8):
                        b.w(v, "s_ctr", 16 * (c + 1))
                        b.w(v, "s_pe", PE0 + 2 + g if g >= 1 else PE0)
                        b.inc("s_dv", v.tensor_tensor(
                            xdg[:].rearrange("p (h q) -> p h q", h=16),
                            xtm[:, g * 1024:(g + 1) * 1024].rearrange("p (h q) -> p h q", h=16),
                            p_ddttm[:, c * 128 + 16 * g:c * 128 + 16 * g + 16]
                                .unsqueeze(2).broadcast_to([128, 16, 64]),
                            ALU.mult))
                        b.w(v, "s_dv")
                        b.inc("s_dv", v.tensor_tensor(
                            tmpc[:].rearrange("p (h q) -> p h q", h=16),
                            p_carry[:, g * 1024:(g + 1) * 1024].rearrange("p (h q) -> p h q", h=16),
                            p_cdbc[:, c * 128 + 16 * g:c * 128 + 16 * g + 16]
                                .unsqueeze(2).broadcast_to([128, 16, 64]),
                            ALU.mult))
                        b.w(v, "s_pe", PE0 + 2 + (g + 1))
                        b.w(v, "s_dv")
                        ps_ = psA if g % 2 == 0 else psB
                        b.inc("s_dv", v.tensor_tensor(p_carry[:, g * 1024:(g + 1) * 1024],
                                                      tmpc[:], ps_[:], ALU.add))
                    assert b.cnt["s_dv"] == DV0 + DVC

                @block.tensor
                def _(t):
                    b.w(t, "s_dv", DV0 + 5)
                    if c >= 1:
                        b.w(t, "s_dv", DV0 - DVC + 8)
                    t.transpose(psE[:, 0:128], p_csfm[:, csl], cs["identf"][:])
                    t.transpose(psE[:, 128:256], p_dtvfm[:, csl], cs["identf"][:])
                    ins = t.transpose(psE[:, 256:384], ddt_sb[:], cs["identf"][:])
                    b.inc("s_pe", ins)
                    b.w(t, "s_ac", AC0 + 5)
                    if c >= 1:
                        b.w(t, "s_dv", DV0 - DVC + 13)
                    b.inc("s_pe", t.matmul(psE[:, 1536:1664], cs["ones1f"][:], cdrow_e[:],
                                           start=True, stop=True))
                    for g in range(8):
                        b.w(t, "s_dv", DV0 + 15 + 3 * g)
                        pg = (c * 8 + g) - 2
                        if pg >= 0:
                            pc, pgg = divmod(pg, 8)
                            b.w(t, "s_dv", (DV_DBC + pc * DVC) + 17 + 3 * pgg)
                        ps_ = psA if g % 2 == 0 else psB
                        for half in range(2):
                            ins = t.matmul(ps_[:, half * 512:(half + 1) * 512],
                                           xtm[:, (64 + g) * 128:(65 + g) * 128],
                                           xdg[:, half * 512:(half + 1) * 512],
                                           start=True, stop=True)
                        b.inc("s_pe", ins)
                    assert b.cnt["s_pe"] == PE0 + PEC

                @block.sync
                def _(sync):
                    b.w(sync, "s_spctm", A_BAR["s_spctm"])
                    if c >= 1:
                        b.w(sync, "s_pe", PE0)
                        b.w(sync, "s_dv", DV0)
                    b.inc("s_ctr", sync.dma_start(
                        out=xtm[:].rearrange("p (f t) -> p f t", f=72),
                        in_=T["sp_ctm"][c].rearrange("f t q -> t f q")), 16)
                    b.w(sync, "s_dv", DV0 + 9)
                    b.inc("s_b0", sync.dma_start(out=T["bnc_cs3"][c, 0], in_=cs3h[:]), 16)
                    b.w(sync, "s_dv", DV0 + 11)
                    b.inc("s_b1", sync.dma_start(out=T["bnc_cs3"][c, 1], in_=lob[:]), 16)
                    b.w(sync, "s_dv", DV0 + 12)
                    b.inc("s_b2", sync.dma_start(out=T["bnc_cs3"][c, 2], in_=lo2[:]), 16)
                    b.w(sync, "s_dv", DV0 + 6)
                    b.w(sync, "s_b4", 16 * c)
                    b.inc("s_b3", sync.dma_start(out=T["bnc"][0:1, 0:128], in_=p_cstm[127:128, csl]), 16)
                    b.w(sync, "s_b3")
                    b.w(sync, "s_ac", AC0)
                    b.inc("s_b4", sync.dma_start(out=cdrow[:], in_=T["bnc"][0:1, 0:128]), 16)

            S_DV_2A = b.cnt["s_dv"]

            @block.sync
            def _(sync):
                b.w(sync, "s_dv", S_DV_2A)
                b.inc("s_ag2s", sync.dma_start(out=T["ag2_s"][:, 0:8192], in_=p_carry[:]), 16)
                b.inc("s_ag2s", sync.dma_start(out=T["ag2_s"][:, 8192:8320], in_=p_cdprod[:]), 16)

            @block.gpsimd
            def _(g):
                b.w(g, "s_ag2s", 32)
                b.inc("s_cc2", g.collective_compute(
                    "AllGather", ALU.bypass, replica_groups=[list(range(8))],
                    ins=[T["ag2_s"]], outs=[T["ag2_r"]]))

            esB2 = contextlib.ExitStack()
            bB2 = Bld(nc, esB2, block)
            esB2.__enter__()
            cdp8 = bB2.sb("cdp8", [128, 8 * 128], F32)
            fj = bB2.sb("fj", [128, 8 * 128], F32)
            ft1 = bB2.sb("ft1", [128, 128], F32)
            srj = [bB2.sb(f"srj{i}", [128, 4096], F32) for i in range(2)]
            srt = bB2.sb("srt", [128, 4096], F32)

            DV_FJ = S_DV_2A + 8 * 25 + 1

            @block.sync
            def _(sync):
                b.w(sync, "s_cc2", 1)
                for j in range(8):
                    b.inc("s_agr", sync.dma_start(out=cdp8[:, j * 128:(j + 1) * 128],
                                                  in_=T["ag2_r"][j * 128:(j + 1) * 128, 8192:8320]), 16)
                for j in range(8):
                    for h in range(2):
                        q = 2 * j + h
                        if q >= 2:
                            b.w(sync, "s_dv", DV_FJ + 2 * (q - 2) + 1)
                        b.inc(f"s_srj{h}", sync.dma_start(
                            out=srj[h][:], in_=T["ag2_r"][j * 128:(j + 1) * 128,
                                                          h * 4096:(h + 1) * 4096]), 16)

            @block.vector
            def _(v):
                b.w(v, "s_agr", 16 * 8)
                for j in range(8):
                    b.inc("s_dv", v.tensor_scalar(fj[:, j * 128:(j + 1) * 128], cs["ones128f"][:],
                                                  cs["mask8e"][:, j:j + 1], None, ALU.mult))
                    for m in range(8):
                        b.inc("s_dv", v.tensor_scalar(ft1[:], cdp8[:, m * 128:(m + 1) * 128],
                                                      -1.0, None, ALU.add))
                        b.w(v, "s_dv")
                        b.inc("s_dv", v.tensor_scalar(ft1[:], ft1[:], cs["fsele"][:, j * 8 + m:j * 8 + m + 1],
                                                      1.0, ALU.mult, ALU.add))
                        b.w(v, "s_dv")
                        b.inc("s_dv", v.tensor_tensor(fj[:, j * 128:(j + 1) * 128],
                                                      fj[:, j * 128:(j + 1) * 128], ft1[:], ALU.mult))
                        b.w(v, "s_dv")
                b.inc("s_dv", v.memset(p_carry[:], 0.0))
                assert b.cnt["s_dv"] == DV_FJ
                for j in range(8):
                    for h in range(2):
                        q = 2 * j + h
                        b.w(v, f"s_srj{h}", 16 * (j + 1))
                        b.inc("s_dv", v.tensor_tensor(
                            srt[:].rearrange("p (h q) -> p h q", h=64),
                            srj[h][:].rearrange("p (h q) -> p h q", h=64),
                            fj[:, j * 128 + h * 64:j * 128 + h * 64 + 64]
                                .unsqueeze(2).broadcast_to([128, 64, 64]),
                            ALU.mult))
                        b.w(v, "s_dv")
                        b.inc("s_dv", v.tensor_tensor(p_carry[:, h * 4096:(h + 1) * 4096],
                                                      p_carry[:, h * 4096:(h + 1) * 4096],
                                                      srt[:], ALU.add))
                        b.w(v, "s_dv")
            DV_CI = b.cnt["s_dv"]

            if PH <= 20:
                @block.gpsimd
                def _(g):
                    b.w(g, "s_dv", DV_CI)
                    if "d_dtv" in DEV:
                        b.inc("s_dev", g.dma_start(out=DEV["d_dtv"], in_=p_dtvfm[:]), 16)
                    if "d_cs" in DEV:
                        b.inc("s_dev", g.dma_start(out=DEV["d_cs"], in_=p_csfm[:]), 16)
                    if "d_carry" in DEV:
                        b.inc("s_dev", g.dma_start(out=DEV["d_carry"], in_=p_carry[:]), 16)
                    if "d_srun" in DEV:
                        b.inc("s_dev", g.dma_start(out=DEV["d_srun"], in_=T["ag2_s"][:, 0:8192]), 16)
                    b.w(g, "s_dev")
                return nc, T

            esB2.__exit__(None, None, None)
            bc_sb = bB.sb("bc_sb", [128, 2048], BF16)
            gate_sb = bB.sb("gate_sb", [128, INTER], BF16)
            cs3_sb = [bB.sb(f"cs3_{i}", [3, 2048], BF16) for i in range(2)]
            arena_a = bB.sb("arena_a", [128, 8192], BF16)
            u_sb = arena_a[:].bitcast(F32)[:, 0:2048]
            scl = arena_a[:, 4096:6144]
            scm = arena_a[:, 6144:8192]
            ygn_sb = arena_a
            dx_sb = bB.sb("dx_sb", [128, 1024], F32)
            ya = bB.sb("ya", [128, 1024], F32)
            yb = bB.sb("yb", [128, 1024], F32)
            y_sb = bB.sb("y_sb", [128, INTER], BF16)
            yf_sb = bB.sb("yf_sb", [128, INTER], BF16)
            cbtm = bB.sb("cbtm", [128, 1024], BF16)
            rstd8 = ya[0:1, :]
            rstd8b = yb[0:1, :]
            rbc = tmpc

            DV_HG = 10
            DV_YGN = 2 + 8 * DV_HG + 8
            PEC2B = 2 + 8 * 5 + 4 + 1 + 1
            ACC2B = 8 + 3
            CTR0 = 16 * 8

            for c in range(N2B):
                csl = slice(c * 128, (c + 1) * 128)
                PE0 = b.cnt["s_pe"]; DV0 = b.cnt["s_dv"]; AC0 = b.cnt["s_ac"]; PO0 = b.cnt["s_po"]

                @block.sync
                def _(sync):
                    if c >= 1:
                        b.w(sync, "s_pe", PE0)
                        b.w(sync, "s_dv", DV0)
                        b.w(sync, "s_po", PO0)
                        b.w(sync, "s_ac", AC0)
                    else:
                        b.w(sync, "s_dv", DV_CI)
                        b.w(sync, "s_pe", PE0)
                    b.w(sync, "s_spcv0", A_BAR["s_spcv0"])
                    b.w(sync, "s_spcv1", A_BAR["s_spcv1"])
                    b.inc("s_ctrb", sync.dma_start(
                        out=bc_sb[:].rearrange("p (f t) -> p f t", f=16),
                        in_=T["sp_conv"][c, 64:80].rearrange("f q t -> q f t")), 16)
                    b.inc("s_ctr", sync.dma_start(
                        out=xtm[:].rearrange("p (f t) -> p f t", f=72),
                        in_=T["sp_ctm"][c].rearrange("f t q -> t f q")), 16)
                    b.w(sync, "s_b0", 16 * 8); b.w(sync, "s_b1", 16 * 8); b.w(sync, "s_b2", 16 * 8)
                    for g in range(8):
                        if g >= 2:
                            b.w(sync, "s_pe", PE0 + 2 + (g - 2) * 5 + 1)
                        b.inc(f"s_cs3r{g % 2}", sync.dma_start(
                            out=cs3_sb[g % 2][:], in_=T["bnc_cs3"][c, :, g * 2048:(g + 1) * 2048]), 16)
                    for sl in range(4):
                        b.w(sync, f"s_sp1_{sl}", A_BAR[f"s_sp1_{sl}"])
                    b.inc("s_gt", sync.dma_start(
                        out=gate_sb[:].rearrange("p (f t) -> p f t", f=64),
                        in_=T["sp_gate"][c].rearrange("f q t -> q f t")), 16)
                    b.w(sync, "s_dv", DV0 + 2 + 8 * DV_HG)
                    b.inc("s_spyg", sync.dma_start(out=T["sp_y"][c], in_=y_sb[:]), 16)
                    b.w(sync, "s_dv", DV0 + DV_YGN)
                    b.inc("s_spyg", sync.dma_start(
                        out=T["sp_ygn"][:, c, :, :].rearrange("f p t -> p f t"),
                        in_=ygn_sb[:].rearrange("p (f t) -> p f t", f=64)), 16)

                @block.vector
                def _(v):
                    if c == 0:
                        b.w(v, "s_dv", DV_CI)
                    else:
                        b.w(v, "s_pe", PE0 - 7)
                        b.w(v, "s_spyg", 32 * c)
                    b.inc("s_dv", v.tensor_copy(p_carrybf[:], p_carry[:]))
                    b.w(v, "s_pe", PE0 + 2)
                    b.inc("s_dv", v.tensor_tensor(
                        cbtm[:].rearrange("p (g l) -> p g l", g=8),
                        psE[:, 0:1024].rearrange("p (g l) -> p g l", g=8),
                        cs["trilbf"][:].unsqueeze(1).broadcast_to([128, 8, 128]),
                        ALU.mult))
                    for g in range(8):
                        D0 = DV0 + 2 + g * DV_HG
                        b.w(v, "s_pe", PE0 + 2 + g * 5 + 1)
                        b.w(v, "s_ac", AC0 + g if g >= 1 else AC0)
                        b.inc("s_dv", v.scalar_tensor_tensor(
                            u_sb[:].rearrange("p (h l) -> p h l", h=16),
                            psE[:].rearrange("p (h l) -> p h l", h=16),
                            -1.0,
                            p_cstm[:, c * 128 + 16 * g:c * 128 + 16 * g + 16]
                                .unsqueeze(2).broadcast_to([128, 16, 128]),
                            ALU.mult, ALU.add))
                        b.w(v, "s_dv")
                        b.inc("s_dv", v.tensor_scalar(u_sb[:], u_sb[:], 0.0, None, ALU.max))
                        b.w(v, "s_ac", AC0 + g + 1)
                        b.w(v, "s_pe", PE0 + 2 + (g - 1) * 5 + 3 if g >= 1 else PE0)
                        b.inc("s_dv", v.tensor_tensor(
                            scm[:].rearrange("p (h l) -> p h l", h=16),
                            scl[:].rearrange("p (h l) -> p h l", h=16),
                            cbtm[:, g * 128:(g + 1) * 128].unsqueeze(1)
                                .broadcast_to([128, 16, 128]),
                            ALU.mult))
                        b.w(v, "s_ctr", CTR0 + 16 * (c + 1))
                        b.inc("s_dv", v.tensor_tensor(
                            xdt_sb[:].rearrange("p (h q) -> p h q", h=16),
                            xtm[:, g * 1024:(g + 1) * 1024].rearrange("p (h q) -> p h q", h=16),
                            p_dtvtm[:, c * 128 + 16 * g:c * 128 + 16 * g + 16]
                                .unsqueeze(2).broadcast_to([128, 16, 64]),
                            ALU.mult))
                        b.inc("s_dv", v.tensor_tensor(
                            xdg[:].rearrange("p (h q) -> p h q", h=16),
                            xtm[:, g * 1024:(g + 1) * 1024].rearrange("p (h q) -> p h q", h=16),
                            p_ddttm[:, c * 128 + 16 * g:c * 128 + 16 * g + 16]
                                .unsqueeze(2).broadcast_to([128, 16, 64]),
                            ALU.mult))
                        b.w(v, "s_dv")
                        b.inc("s_dv", v.tensor_tensor(
                            tmpc[:].rearrange("p (h q) -> p h q", h=16),
                            p_carry[:, g * 1024:(g + 1) * 1024].rearrange("p (h q) -> p h q", h=16),
                            p_cdbc[:, c * 128 + 16 * g:c * 128 + 16 * g + 16]
                                .unsqueeze(2).broadcast_to([128, 16, 64]),
                            ALU.mult))
                        b.w(v, "s_pe", PE0 + 2 + g * 5 + 2)
                        b.w(v, "s_dv")
                        b.inc("s_dv", v.tensor_tensor(p_carry[:, g * 1024:(g + 1) * 1024],
                                                      tmpc[:], psE[:, 0:1024], ALU.add))
                        b.w(v, "s_pe", PE0 + 2 + g * 5 + 5)
                        b.w(v, "s_po", PO0 + g + 1)
                        b.inc("s_dv", v.tensor_tensor(
                            ya[:].rearrange("p (h q) -> p h q", h=16),
                            psB[:].rearrange("p (h q) -> p h q", h=16),
                            p_ecstm[:, c * 128 + 16 * g:c * 128 + 16 * g + 16]
                                .unsqueeze(2).broadcast_to([128, 16, 64]),
                            ALU.mult))
                        b.w(v, "s_dv")
                        b.inc("s_dv", v.tensor_tensor(yb[:], ya[:], psA[:], ALU.add))
                        b.w(v, "s_dv")
                        b.inc("s_dv", v.tensor_tensor(y_sb[:, g * 1024:(g + 1) * 1024],
                                                      yb[:], dx_sb[:], ALU.add))
                        assert b.cnt["s_dv"] == D0 + DV_HG
                    DY0 = DV0 + 2 + 8 * DV_HG
                    b.w(v, "s_ac", AC0)
                    for q in range(4):
                        b.w(v, "s_pe", PE0 + 42 + q + 1)
                        b.inc("s_dv", v.tensor_copy(yf_sb[:, q * 2048:(q + 1) * 2048],
                                                    psE[:, 0:1024].bitcast(BF16)[:, 0:2048]))
                    b.w(v, "s_gt", 16 * (c + 1))
                    b.w(v, "s_dv")
                    b.inc("s_dv", v.tensor_tensor(yf_sb[:], yf_sb[:], gate_sb[:], ALU.mult))
                    b.w(v, "s_pe", PE0 + 42 + 4 + 2)
                    b.inc("s_dv", v.tensor_copy(rbc[:], psE[:, 1024:2048]))
                    b.w(v, "s_dv")
                    b.inc("s_dv", v.tensor_tensor(
                        ygn_sb[:].rearrange("p (g r t) -> p g r t", g=8, r=8),
                        yf_sb[:].rearrange("p (g r t) -> p g r t", g=8, r=8),
                        rbc[:].rearrange("p (g t) -> p g t", g=8).unsqueeze(2).broadcast_to([128, 8, 8, 128]),
                        ALU.mult))
                    b.w(v, "s_dv")
                    b.inc("s_dv", v.tensor_tensor(
                        ygn_sb[:].rearrange("p (f t) -> p f t", f=64),
                        ygn_sb[:].rearrange("p (f t) -> p f t", f=64),
                        cs["norm_w"][:].unsqueeze(2).broadcast_to([128, 64, 128]),
                        ALU.mult))
                    assert b.cnt["s_dv"] == DV0 + DV_YGN

                @block.scalar
                def _(s):
                    for g in range(8):
                        D0 = DV0 + 2 + g * DV_HG
                        b.w(s, "s_dv", D0 + 2)
                        b.inc("s_ac", s.activation(scl[:], u_sb[:], AF.Exp, bias=0.0, scale=-1.0))
                    b.w(s, "s_dv", DV0 + 2 + 8 * DV_HG + 5)
                    b.w(s, "s_spyg", 32 * c + 16)
                    b.inc("s_ac", s.activation(y_sb[:], yf_sb[:], AF.Square))
                    b.w(s, "s_pe", PE0 + 42 + 4 + 1)
                    b.inc("s_ac", s.activation(rstd8[:], psE[0:1, 1024:2048], AF.Ln,
                                               bias=cs["eps8"][0:1, :], scale=1.0 / 1024.0))
                    b.w(s, "s_ac")
                    b.inc("s_ac", s.activation(rstd8b[:], rstd8[:], AF.Exp, bias=0.0, scale=-0.5))
                    assert b.cnt["s_ac"] == AC0 + ACC2B

                @block.gpsimd
                def _(g_):
                    for g in range(8):
                        b.w(g_, "s_ctr", CTR0 + 16 * (c + 1))
                        if g >= 1:
                            b.w(g_, "s_dv", DV0 + 2 + (g - 1) * DV_HG + DV_HG)
                        elif c >= 1:
                            b.w(g_, "s_dv", DV0)
                        else:
                            b.w(g_, "s_dv", DV_DBC)
                        b.inc("s_po", g_.tensor_tensor(
                            dx_sb[:].rearrange("p (h q) -> p h q", h=16),
                            xtm[:, g * 1024:(g + 1) * 1024].rearrange("p (h q) -> p h q", h=16),
                            cs["Dbc"][:, 16 * g:16 * g + 16]
                                .unsqueeze(2).broadcast_to([128, 16, 64]),
                            ALU.mult))

                @block.tensor
                def _(t):
                    b.w(t, "s_ctrb", 16 * (c + 1))
                    if c >= 1:
                        b.w(t, "s_dv", DV0)
                    for g in range(8):
                        ins = t.matmul(psE[:, g * 128:(g + 1) * 128],
                                       bc_sb[:, g * 128:(g + 1) * 128],
                                       bc_sb[:, 1024 + g * 128:1024 + (g + 1) * 128],
                                       start=True, stop=True)
                    b.inc("s_pe", ins, 2)
                    for g in range(8):
                        G0 = PE0 + 2 + g * 5
                        b.w(t, f"s_cs3r{g % 2}", 16 * ((8 * c + g) // 2 + 1))
                        if g == 0:
                            b.w(t, "s_dv", DV0 + 2)
                        else:
                            b.w(t, "s_dv", DV0 + 2 + (g - 1) * DV_HG + 7)
                        for q in range(4):
                            ins = t.matmul(psE[:, q * 512:(q + 1) * 512], cs["ones3bf"][:],
                                           cs3_sb[g % 2][:, q * 512:(q + 1) * 512],
                                           start=True, stop=True)
                        b.inc("s_pe", ins)
                        b.w(t, "s_dv", DV0 + 2 + g * DV_HG + 5)
                        for half in range(2):
                            ins = t.matmul(psE[:, half * 512:(half + 1) * 512],
                                           xtm[:, (64 + g) * 128:(65 + g) * 128],
                                           xdg[:, half * 512:(half + 1) * 512],
                                           start=True, stop=True)
                        b.inc("s_pe", ins)
                        b.w(t, "s_dv", DV0 + 2 + g * DV_HG + 4)
                        if g >= 1:
                            b.w(t, "s_dv", DV0 + 2 + (g - 1) * DV_HG + 9)
                        for i in range(16):
                            ins = t.matmul(psA[:, i * 64:(i + 1) * 64],
                                           scm[:, i * 128:(i + 1) * 128],
                                           xdt_sb[:, i * 64:(i + 1) * 64],
                                           start=True, stop=True)
                        b.inc("s_pe", ins)
                        b.w(t, "s_dv", DV0 + 1)
                        if g >= 1:
                            b.w(t, "s_dv", DV0 + 2 + (g - 1) * DV_HG + 8)
                        for i in range(8):
                            ins = t.matmul(psB[:, i * 128:(i + 1) * 128],
                                           bc_sb[:, 1024 + g * 128:1024 + (g + 1) * 128],
                                           p_carrybf[:, g * 1024 + i * 128:g * 1024 + (i + 1) * 128],
                                           start=True, stop=True)
                        b.inc("s_pe", ins, 2)
                    for q in range(4):
                        b.w(t, "s_dv", DV0 + 2 + 8 * DV_HG)
                        if q >= 1:
                            b.w(t, "s_dv", DV0 + 2 + 8 * DV_HG + q)
                        psEbf = psE[:, 0:1024].bitcast(BF16)
                        for i in range(16):
                            ins = t.transpose(psEbf[:, i * 128:(i + 1) * 128],
                                              y_sb[:, (q * 16 + i) * 128:(q * 16 + i + 1) * 128],
                                              cs["identbf"][:])
                        b.inc("s_pe", ins)
                    b.w(t, "s_ac", AC0 + 8 + 1)
                    for ftile in range(64):
                        g = ftile // 8
                        ins = t.matmul(psE[0:1, 1024 + g * 128:1024 + (g + 1) * 128],
                                       cs["egsel"][:, 0:1],
                                       y_sb[:, ftile * 128:(ftile + 1) * 128],
                                       start=(ftile % 8 == 0), stop=(ftile % 8 == 7))
                    b.inc("s_pe", ins)
                    b.w(t, "s_ac", AC0 + ACC2B)
                    for g in range(8):
                        ins = t.matmul(psE[:, 1024 + g * 128:1024 + (g + 1) * 128],
                                       cs["ones1f"][:], rstd8b[0:1, g * 128:(g + 1) * 128],
                                       start=True, stop=True)
                    b.inc("s_pe", ins)
                    assert b.cnt["s_pe"] == PE0 + PEC2B

            if PH <= 25:
                @block.gpsimd
                def _(g):
                    b.w(g, "s_spyg", 32 * N2B)
                    if "d_y" in DEV:
                        for c_ in range(N2B):
                            b.inc("s_dev", g.dma_start(out=DEV["d_y"][c_ * 128:(c_ + 1) * 128, :],
                                                       in_=T["sp_y"][c_]), 16)
                    b.w(g, "s_dv")
                    if "d_ygn" in DEV:
                        for f in range(64):
                            b.inc("s_dev", g.dma_start(
                                out=DEV["d_ygn"][f * 128:(f + 1) * 128, :].rearrange("p (c t) -> p c t", c=8),
                                in_=T["sp_ygn"][f].rearrange("c p t -> p c t")), 16)
                    if "d_carry" in DEV:
                        b.inc("s_dev", g.dma_start(out=DEV["d_carry"], in_=p_carry[:]), 16)
                    b.w(g, "s_dev")
                return nc, T

        B_BAR = {s_: b.cnt[s_] for s_ in ["s_pe", "s_dv", "s_ac", "s_po", "s_spyg"]}
        b.sem("s_w20", "s_w21", "s_ygr", "s_xr0", "s_xr1",
              "s_spr0", "s_spr1", "s_r2r0", "s_r2r1", "s_sxf0", "s_sxf1",
              "s_sxb0", "s_sxb1", "s_rwl", "s_atl", "s_x2r0", "s_x2r1",
              "s_xf2r0", "s_xf2r1", "s_lnb", "s_outw")
        esS = contextlib.ExitStack()
        bS = Bld(nc, esS, block)
        with esS:
            x2fb = [bS.sb(f"x2fb{i}", [128, TOK], F32) for i in range(2)]
            x2bb = [bS.sb(f"x2bb{i}", [128, TOK], BF16) for i in range(2)]
            wrow_bf = bS.sb("wrow_bf", [8, TOK], BF16)
            w_all = bS.sb("w_all", [128, 64], F32)
            ln1 = bS.sb("ln1", [1, 4 * TOK], F32)

            for eng_name in ["tensor", "vector", "scalar", "sync", "gpsimd"]:
                def _bar(e):
                    for s_, v_ in B_BAR.items():
                        e.wait_ge(b.sems[s_], v_)
                getattr(block, eng_name)(_bar)

            PE_C0 = b.cnt["s_pe"]; DV_C0 = b.cnt["s_dv"]; AC_C0 = b.cnt["s_ac"]
            pe_main, pe_stat = {}, {}
            c_ = PE_C0
            for d in range(32):
                c_ += 1; pe_main[d] = c_
                if d >= 1:
                    c_ += 1; pe_stat[d - 1] = c_
            c_ += 1; pe_stat[31] = c_
            PE_BC1 = c_ + 1; PE_BC2 = c_ + 2
            pe_rt = {d: PE_BC2 + d + 1 for d in range(32)}
            PE_LTR = PE_BC2 + 32 + 1
            PE_WTR = PE_LTR + 1
            DV_EV = DV_C0 + 32
            DV_LN = DV_EV + 4
            dv_x2 = {d: DV_LN + 3 * (d + 1) for d in range(32)}
            DV_SM0 = DV_LN + 96
            DV_W = DV_SM0 + 14 * 8
            DV_WROW = DV_W + 1
            AC_SQ = {d: AC_C0 + d + 1 for d in range(32)}
            AC_LNa = AC_C0 + 33
            AC_LNb = AC_C0 + 34
            AC_LG = AC_C0 + 35
            AC_SME = {tt: AC_C0 + 35 + tt + 1 for tt in range(8)}

            esC = contextlib.ExitStack()
            bC = Bld(nc, esC, block)
            esC.__enter__()
            ygr = bC.sb("ygr", [128, 64 * 1024], BF16)
            w2sl = [bC.sb(f"w2sl{i}", [128, 32 * 128], BF16) for i in range(2)]
            xrs = [bC.sb(f"xrs{i}", [128, TOK], F32) for i in range(2)]
            r2st = [bC.sb(f"r2st{i}", [128, TOK], F32) for i in range(2)]
            r2sq = bC.sb("r2sq", [128, TOK], BF16)
            rwt = bC.sb("rwt", [128, 256], F32)
            logit_sb = bC.sb("logit_sb", [8, TOK], F32)
            sm = [bC.sb(f"sm{i}", [128, 8], F32) for i in range(6)]
            smc = bC.sb("smc", [128, 6], F32)

            @block.gpsimd
            def _(g):
                b.inc("s_rwl", g.dma_start(out=rwt[:].rearrange("p (a c) -> p a c", a=32),
                                           in_=T["router_wT"].rearrange("a p c -> p a c")), 16)
                for d in range(32):
                    for hh in range(2):
                        q = 2 * d + hh
                        if q >= 2:
                            b.w(g, "s_pe", pe_main[(q - 2) // 2])
                        b.inc(f"s_w2{q % 2}", g.dma_start(
                            out=w2sl[q % 2][:].rearrange("p (a c) -> p a c", a=32),
                            in_=T["W2"][d, hh * 32:(hh + 1) * 32].rearrange("a p c -> p a c")), 16)

            @block.sync
            def _(sync):
                for f in range(64):
                    b.inc("s_ygr", sync.dma_start(
                        out=ygr[:, f * 1024:(f + 1) * 1024].rearrange("p (c t) -> p c t", c=8),
                        in_=T["sp_ygn"][f].rearrange("c p t -> p c t")), 16)
                for d in range(32):
                    if d >= 2:
                        b.w(sync, "s_dv", DV_C0 + (d - 2) + 1)
                    b.inc(f"s_xr{d % 2}", sync.dma_start(
                        out=xrs[d % 2][:], in_=T["xT"][d * 128:(d + 1) * 128, :]), 16)
                    b.w(sync, "s_dv", DV_C0 + d + 1)
                    b.inc(f"s_spr{d % 2}", sync.dma_start(out=T["sp_r2"][d], in_=r2st[d % 2][:]), 16)
                b.w(sync, "s_pe", pe_stat[31])
                b.w(sync, "s_spr0", 16 * 16)
                b.w(sync, "s_spr1", 16 * 16)
                for d in range(32):
                    if d >= 2:
                        b.w(sync, "s_dv", dv_x2[d - 2])
                    b.inc(f"s_r2r{d % 2}", sync.dma_start(out=r2st[d % 2][:], in_=T["sp_r2"][d]), 16)
                    b.w(sync, "s_dv", dv_x2[d])
                    b.inc(f"s_sxf{d % 2}", sync.dma_start(out=T["sp_x2f"][d], in_=x2fb[d % 2][:]), 16)
                    b.inc(f"s_sxb{d % 2}", sync.dma_start(out=T["sp_x2b"][d], in_=x2bb[d % 2][:]), 16)

            @block.tensor
            def _(t):
                b.w(t, "s_ygr", 16 * 64)

                def stats(d):
                    b.w(t, "s_dv", DV_C0 + d + 1)
                    for cg in range(2):
                        t.matmul(psE[0:1, cg * 512:(cg + 1) * 512],
                                 cs["ones128f"][:, 0:1],
                                 r2st[d % 2][:, cg * 512:(cg + 1) * 512],
                                 start=(d == 0), stop=(d == 31))
                    b.w(t, "s_ac", AC_SQ[d])
                    for cg in range(2):
                        ins = t.matmul(psE[0:1, 1024 + cg * 512:1024 + (cg + 1) * 512],
                                       cs["egsel"][:, 0:1],
                                       r2sq[:, cg * 512:(cg + 1) * 512],
                                       start=(d == 0), stop=(d == 31))
                    b.inc("s_pe", ins)
                    assert b.cnt["s_pe"] == pe_stat[d]

                for d in range(32):
                    if d >= 2:
                        b.w(t, "s_dv", DV_C0 + (d - 2) + 1)
                    ps_ = psA if d % 2 == 0 else psB
                    for hh in range(2):
                        q = 2 * d + hh
                        b.w(t, f"s_w2{q % 2}", 16 * (q // 2 + 1))
                        for k in range(32):
                            kk = hh * 32 + k
                            for cg in range(2):
                                ins = t.matmul(ps_[:, cg * 512:(cg + 1) * 512],
                                               w2sl[q % 2][:, k * 128:(k + 1) * 128],
                                               ygr[:, kk * 1024 + cg * 512:kk * 1024 + cg * 512 + 512],
                                               start=(kk == 0), stop=(kk == 63))
                    b.inc("s_pe", ins)
                    assert b.cnt["s_pe"] == pe_main[d]
                    if d >= 1:
                        stats(d - 1)
                stats(31)
                b.w(t, "s_dv", DV_LN)
                for cg in range(2):
                    ins = t.matmul(psE[:, cg * 512:(cg + 1) * 512], cs["ones1f"][:],
                                   ln1[0:1, 2 * TOK + cg * 512:2 * TOK + (cg + 1) * 512],
                                   start=True, stop=True)
                b.inc("s_pe", ins)
                for cg in range(2):
                    ins = t.matmul(psE[:, 1024 + cg * 512:1024 + (cg + 1) * 512], cs["ones1f"][:],
                                   ln1[0:1, 3 * TOK + cg * 512:3 * TOK + (cg + 1) * 512],
                                   start=True, stop=True)
                b.inc("s_pe", ins)
                assert b.cnt["s_pe"] == PE_BC2
                b.w(t, "s_rwl", 16)
                for d in range(32):
                    b.w(t, "s_dv", dv_x2[d] - 1)
                    for cg in range(2):
                        ins = t.matmul(psA[0:8, cg * 512:(cg + 1) * 512],
                                       rwt[:, d * 8:(d + 1) * 8],
                                       x2fb[d % 2][:, cg * 512:(cg + 1) * 512],
                                       start=(d == 0), stop=(d == 31))
                    b.inc("s_pe", ins)
                    assert b.cnt["s_pe"] == pe_rt[d]
                b.w(t, "s_ac", AC_LG)
                for tt in range(8):
                    ins = t.transpose(psB[:, tt * 8:(tt + 1) * 8],
                                      logit_sb[:, tt * 128:(tt + 1) * 128], cs["identf"][0:8, 0:8])
                b.inc("s_pe", ins)
                b.w(t, "s_dv", DV_W)
                for tt in range(8):
                    ins = t.transpose(psE[0:8, tt * 128:(tt + 1) * 128],
                                      w_all[:, tt * 8:(tt + 1) * 8], cs["identf"][:])
                b.inc("s_pe", ins)
                assert b.cnt["s_pe"] == PE_WTR

            @block.scalar
            def _(s):
                for d in range(32):
                    b.w(s, "s_dv", DV_C0 + d + 1)
                    if d >= 1:
                        b.w(s, "s_pe", pe_stat[d - 1])
                    b.inc("s_ac", s.activation(r2sq[:], r2st[d % 2][:], AF.Square))
                b.w(s, "s_dv", DV_EV + 3)
                b.inc("s_ac", s.activation(ln1[0:1, 1024:2048], ln1[0:1, 2048:3072], AF.Ln,
                                           bias=cs["eps8"][0:1, :], scale=1.0))
                b.w(s, "s_ac")
                b.inc("s_ac", s.activation(ln1[0:1, 2048:3072], ln1[0:1, 1024:2048], AF.Exp,
                                           bias=0.0, scale=-0.5))
                assert b.cnt["s_ac"] == AC_LNb
                b.w(s, "s_pe", pe_rt[31])
                b.inc("s_ac", s.activation(logit_sb[:], psA[0:8, :], AF.Identity,
                                           bias=cs["router_b"][:], scale=1.0))
                assert b.cnt["s_ac"] == AC_LG
                for tt in range(8):
                    b.w(s, "s_dv", DV_SM0 + 14 * tt + 1)
                    b.inc("s_ac", s.activation(sm[0][:], psB[:, tt * 8:(tt + 1) * 8], AF.Exp,
                                               bias=smc[:, 0:1], scale=1.0))
                    assert b.cnt["s_ac"] == AC_SME[tt]

            @block.vector
            def _(v):
                for d in range(32):
                    b.w(v, "s_pe", pe_main[d])
                    b.w(v, f"s_xr{d % 2}", 16 * (d // 2 + 1))
                    if d >= 2:
                        b.w(v, f"s_spr{d % 2}", 16 * (d // 2))
                    ps_ = psA if d % 2 == 0 else psB
                    b.inc("s_dv", v.tensor_tensor(r2st[d % 2][:], ps_[:], xrs[d % 2][:], ALU.add))
                assert b.cnt["s_dv"] == DV_EV
                b.w(v, "s_pe", pe_stat[31])
                b.inc("s_dv", v.tensor_scalar(ln1[0:1, 0:1024], psE[0:1, 0:1024],
                                              1.0 / 4096.0, None, ALU.mult))
                b.w(v, "s_dv")
                b.inc("s_dv", v.tensor_tensor(ln1[0:1, 1024:2048], ln1[0:1, 0:1024],
                                              ln1[0:1, 0:1024], ALU.mult))
                b.w(v, "s_dv")
                b.inc("s_dv", v.scalar_tensor_tensor(ln1[0:1, 2048:3072], psE[0:1, 1024:2048],
                                                     1.0 / 4096.0, ln1[0:1, 1024:2048],
                                                     ALU.mult, ALU.subtract))
                b.w(v, "s_ac", AC_LNb)
                b.inc("s_dv", v.tensor_tensor(ln1[0:1, 3072:4096], ln1[0:1, 0:1024],
                                              ln1[0:1, 2048:3072], ALU.mult))
                assert b.cnt["s_dv"] == DV_LN
                for d in range(32):
                    b.w(v, f"s_r2r{d % 2}", 16 * (d // 2 + 1))
                    b.w(v, "s_pe", PE_BC2)
                    if d >= 2:
                        b.w(v, f"s_sxf{d % 2}", 16 * (d // 2))
                        b.w(v, f"s_sxb{d % 2}", 16 * (d // 2))
                        b.w(v, "s_pe", pe_rt[d - 2])
                    b.inc("s_dv", v.tensor_tensor(x2fb[d % 2][:], r2st[d % 2][:],
                                                  psE[:, 0:1024], ALU.mult))
                    b.w(v, "s_dv")
                    b.inc("s_dv", v.tensor_tensor(x2fb[d % 2][:], x2fb[d % 2][:],
                                                  psE[:, 1024:2048], ALU.subtract))
                    b.w(v, "s_dv")
                    b.inc("s_dv", v.tensor_copy(x2bb[d % 2][:], x2fb[d % 2][:]))
                    assert b.cnt["s_dv"] == dv_x2[d]
                for tt in range(8):
                    b.w(v, "s_pe", PE_LTR)
                    b.w(v, "s_dv")
                    b.inc("s_dv", v.reduce_max(smc[:, 0:1], psB[:, tt * 8:(tt + 1) * 8], AX,
                                               negate=True))
                    b.w(v, "s_ac", AC_SME[tt])
                    b.inc("s_dv", v.reduce_sum(smc[:, 1:2], sm[0][:], AX))
                    b.w(v, "s_dv")
                    b.inc("s_dv", v.reciprocal(smc[:, 2:3], smc[:, 1:2]))
                    b.w(v, "s_dv")
                    b.inc("s_dv", v.tensor_scalar(sm[1][:], sm[0][:], smc[:, 2:3], None, ALU.mult))
                    b.w(v, "s_dv")
                    b.inc("s_dv", v.reduce_max(smc[:, 3:4], sm[1][:], AX))
                    b.w(v, "s_dv")
                    b.inc("s_dv", v.tensor_tensor(sm[2][:], sm[1][:],
                                                  smc[:, 3:4].broadcast_to([128, 8]), ALU.is_ge))
                    b.w(v, "s_dv")
                    b.inc("s_dv", v.tensor_scalar(sm[3][:], sm[2][:], -1.0, 1.0, ALU.mult, ALU.add))
                    b.w(v, "s_dv")
                    b.inc("s_dv", v.tensor_tensor(sm[4][:], sm[1][:], sm[3][:], ALU.mult))
                    b.w(v, "s_dv")
                    b.inc("s_dv", v.reduce_max(smc[:, 4:5], sm[4][:], AX))
                    b.w(v, "s_dv")
                    b.inc("s_dv", v.tensor_tensor(sm[5][:], sm[4][:],
                                                  smc[:, 4:5].broadcast_to([128, 8]), ALU.is_ge))
                    b.w(v, "s_dv")
                    b.inc("s_dv", v.tensor_scalar(smc[:, 5:6], smc[:, 3:4], smc[:, 4:5], None, ALU.add))
                    b.w(v, "s_dv")
                    b.inc("s_dv", v.reciprocal(smc[:, 0:1], smc[:, 5:6]))
                    b.w(v, "s_dv")
                    b.inc("s_dv", v.tensor_tensor(sm[3][:], sm[2][:], sm[5][:], ALU.add))
                    b.w(v, "s_dv")
                    b.inc("s_dv", v.scalar_tensor_tensor(w_all[:, tt * 8:(tt + 1) * 8],
                                                         sm[1][:], smc[:, 0:1], sm[3][:],
                                                         ALU.mult, ALU.mult))
                assert b.cnt["s_dv"] == DV_W
                b.w(v, "s_pe", PE_WTR)
                b.inc("s_dv", v.tensor_copy(wrow_bf[:], psE[0:8, 0:1024]))
                assert b.cnt["s_dv"] == DV_WROW

            esC.__exit__(None, None, None)

            if PH <= 28:
                @block.gpsimd
                def _(g):
                    b.w(g, "s_dv", DV_WROW)
                    for sl in range(2):
                        b.w(g, f"s_sxf{sl}", 16 * 16)
                        b.w(g, f"s_sxb{sl}", 16 * 16)
                    if "d_x2" in DEV:
                        b.inc("s_dev", g.dma_start(
                            out=DEV["d_x2"].rearrange("(a p) t -> a p t", p=128),
                            in_=T["sp_x2f"]), 16)
                    if "d_w" in DEV:
                        b.inc("s_dev", g.dma_start(
                            out=DEV["d_w"].rearrange("(a p) e -> p a e", p=128),
                            in_=w_all[:].rearrange("p (a e) -> p a e", a=8)), 16)
                    b.w(g, "s_dev")
                return nc, T

            esD = contextlib.ExitStack()
            bD = Bld(nc, esD, block)
            with esD:
                C_BAR = dict(s_pe=PE_WTR, s_dv=DV_WROW, s_ac=AC_SME[7])
                for eng_name in ["tensor", "vector", "scalar", "sync", "gpsimd"]:
                    def _bar(e):
                        for s_, v_ in C_BAR.items():
                            e.wait_ge(b.sems[s_], v_)
                    getattr(block, eng_name)(_bar)
                r3 = bD.sb("r3", [128, 32 * TOK], F32)
                at_bf = bD.sb("at_bf", [128, 4096], BF16)
                bt_bf = bD.sb("bt_bf", [128, 4096], BF16)
                hw_bf = bD.sb("hw_bf", [128, TOK], BF16)
                wbc = bD.sb("wbc", [128, TOK], BF16)
                rcol = bD.sb("rcol", [128, 8], F32)
                mcol = bD.sb("mcol", [128, 8], F32)
                outsb = bD.sb("outsb", [128, 4096], F32)
                r3sq = bD.sb("r3sq", [128, TOK], BF16)
                ln2 = ln1

                PE_D0 = b.cnt["s_pe"]; DV_D0 = b.cnt["s_dv"]; AC_D0 = b.cnt["s_ac"]
                pe_ap = {k: PE_D0 + k + 1 for k in range(32)}
                PE_WEXP = PE_D0 + 33
                pe_bd, pe_st2 = {}, {}
                c_ = PE_WEXP
                for d in range(32):
                    c_ += 1; pe_bd[d] = c_
                    if d >= 1:
                        c_ += 1; pe_st2[d - 1] = c_
                c_ += 1; pe_st2[31] = c_
                pe_otr = {}
                for tt in range(8):
                    for wv in range(2):
                        c_ += 1; pe_otr[(tt, wv)] = c_
                DV_HW = DV_D0 + 2
                dv_r3 = {d: DV_HW + d + 1 for d in range(32)}
                DV_LN2 = DV_HW + 32 + 5
                AC_SQ2 = {d: AC_D0 + d + 1 for d in range(32)}
                AC_LN2b = AC_D0 + 34
                ac_out = {}
                c_ = AC_LN2b
                for tt in range(8):
                    for wv in range(2):
                        c_ += 1; ac_out[(tt, wv)] = c_

                @block.gpsimd
                def _(g):
                    b.inc("s_atl", g.dma_start(out=at_bf[:].rearrange("p (a c) -> p a c", a=32),
                                               in_=T["AT"].rearrange("a p c -> p a c")), 16)
                    b.inc("s_atl", g.dma_start(out=bt_bf[:], in_=T["BT"]), 16)

                @block.sync
                def _(sync):
                    for sl in range(2):
                        b.w(sync, f"s_sxb{sl}", 16 * 16)
                        b.w(sync, f"s_sxf{sl}", 16 * 16)
                    b.w(sync, "s_pe", PE_WTR)
                    for k in range(32):
                        if k >= 2:
                            b.w(sync, "s_pe", pe_ap[k - 2])
                        b.inc(f"s_x2r{k % 2}", sync.dma_start(out=x2bb[k % 2][:],
                                                              in_=T["sp_x2b"][k]), 16)
                    for d in range(32):
                        if d >= 2:
                            b.w(sync, "s_dv", dv_r3[d - 2])
                        b.inc(f"s_xf2r{d % 2}", sync.dma_start(out=x2fb[d % 2][:],
                                                               in_=T["sp_x2f"][d]), 16)
                    b.w(sync, "s_dv", DV_LN2)
                    b.inc("s_lnb", sync.dma_start(out=T["bnc"][2:3, 0:1024], in_=ln2[0:1, 2048:3072]), 16)
                    b.inc("s_lnb", sync.dma_start(out=T["bnc"][3:4, 0:1024], in_=ln2[0:1, 3072:4096]), 16)
                    b.w(sync, "s_lnb", 32)
                    for tt in range(8):
                        b.inc("s_lnb", sync.dma_start(out=rcol[:, tt:tt + 1],
                                                      in_=T["bnc"][2, tt * 128:(tt + 1) * 128].unsqueeze(1)), 16)
                        b.inc("s_lnb", sync.dma_start(out=mcol[:, tt:tt + 1],
                                                      in_=T["bnc"][3, tt * 128:(tt + 1) * 128].unsqueeze(1)), 16)
                    for tt in range(8):
                        b.w(sync, "s_ac", ac_out[(tt, 1)])
                        b.inc("s_outw", sync.dma_start(out=T["out"][tt * 128:(tt + 1) * 128, :],
                                                       in_=outsb[:]), 16)
                    b.w(sync, "s_outw", 16 * 8)

                @block.tensor
                def _(t):
                    b.w(t, "s_atl", 32)
                    for k in range(32):
                        b.w(t, f"s_x2r{k % 2}", 16 * (k // 2 + 1))
                        for cg in range(2):
                            ins = t.matmul(psA[:, cg * 512:(cg + 1) * 512],
                                           at_bf[:, k * 128:(k + 1) * 128],
                                           x2bb[k % 2][:, cg * 512:(cg + 1) * 512],
                                           start=(k == 0), stop=(k == 31))
                        b.inc("s_pe", ins)
                        assert b.cnt["s_pe"] == pe_ap[k]
                    b.w(t, "s_dv", DV_WROW)
                    for cg in range(2):
                        ins = t.matmul(psB[:, cg * 512:(cg + 1) * 512], cs["rsel"][:],
                                       wrow_bf[:, cg * 512:(cg + 1) * 512],
                                       start=True, stop=True)
                    b.inc("s_pe", ins)
                    assert b.cnt["s_pe"] == PE_WEXP

                    def stats2(d):
                        b.w(t, "s_ac", AC_SQ2[d])
                        for cg in range(2):
                            t.matmul(psE[0:1, cg * 512:(cg + 1) * 512],
                                     cs["ones128f"][:, 0:1],
                                     r3[:, d * 1024 + cg * 512:d * 1024 + cg * 512 + 512],
                                     start=(d == 0), stop=(d == 31))
                            ins = t.matmul(psE[0:1, 1024 + cg * 512:1024 + (cg + 1) * 512],
                                           cs["egsel"][:, 0:1],
                                           r3sq[:, cg * 512:(cg + 1) * 512],
                                           start=(d == 0), stop=(d == 31))
                        b.inc("s_pe", ins)
                        assert b.cnt["s_pe"] == pe_st2[d]

                    b.w(t, "s_dv", DV_HW)
                    for d in range(32):
                        if d >= 2:
                            b.w(t, "s_dv", dv_r3[d - 2])
                        ps_ = psA if d % 2 == 0 else psB
                        for cg in range(2):
                            ins = t.matmul(ps_[:, cg * 512:(cg + 1) * 512],
                                           bt_bf[:, d * 128:(d + 1) * 128],
                                           hw_bf[:, cg * 512:(cg + 1) * 512],
                                           start=True, stop=True)
                        b.inc("s_pe", ins)
                        assert b.cnt["s_pe"] == pe_bd[d]
                        if d >= 1:
                            stats2(d - 1)
                    stats2(31)
                    for tt in range(8):
                        for wv in range(2):
                            b.w(t, "s_dv", DV_LN2)
                            if (tt, wv) != (0, 0):
                                prev = (tt, 0) if wv == 1 else (tt - 1, 1)
                                b.w(t, "s_ac", ac_out[prev])
                            for i in range(16):
                                dd = wv * 16 + i
                                ins = t.transpose(psE[:, i * 128:(i + 1) * 128],
                                                  r3[:, dd * 1024 + tt * 128:dd * 1024 + (tt + 1) * 128],
                                                  cs["identf"][:])
                            b.inc("s_pe", ins)
                            assert b.cnt["s_pe"] == pe_otr[(tt, wv)]

                @block.scalar
                def _(s):
                    for d in range(32):
                        b.w(s, "s_dv", dv_r3[d])
                        if d >= 1:
                            b.w(s, "s_pe", pe_st2[d - 1])
                        b.inc("s_ac", s.activation(r3sq[:], r3[:, d * 1024:(d + 1) * 1024], AF.Square))
                    b.w(s, "s_dv", DV_HW + 32 + 3)
                    b.inc("s_ac", s.activation(ln2[0:1, 1024:2048], ln2[0:1, 2048:3072], AF.Ln,
                                               bias=cs["eps8"][0:1, :], scale=1.0))
                    b.w(s, "s_ac")
                    b.inc("s_ac", s.activation(ln2[0:1, 2048:3072], ln2[0:1, 1024:2048], AF.Exp,
                                               bias=0.0, scale=-0.5))
                    assert b.cnt["s_ac"] == AC_LN2b
                    for tt in range(8):
                        for wv in range(2):
                            b.w(s, "s_pe", pe_otr[(tt, wv)])
                            b.w(s, "s_lnb", 32 + 16 * 16)
                            if wv == 0 and tt >= 1:
                                b.w(s, "s_outw", 16 * tt)
                            for i in range(16):
                                dd = wv * 16 + i
                                ins = s.activation(outsb[:, dd * 128:(dd + 1) * 128],
                                                   psE[:, i * 128:(i + 1) * 128], AF.Identity,
                                                   bias=mcol[:, tt:tt + 1], scale=rcol[:, tt:tt + 1])
                            b.inc("s_ac", ins)
                            assert b.cnt["s_ac"] == ac_out[(tt, wv)]

                @block.vector
                def _(v):
                    b.w(v, "s_pe", PE_WEXP)
                    b.inc("s_dv", v.tensor_copy(wbc[:], psB[:]))
                    b.w(v, "s_dv")
                    b.inc("s_dv", v.tensor_tensor(hw_bf[:], psA[:], wbc[:], ALU.mult))
                    assert b.cnt["s_dv"] == DV_HW
                    for d in range(32):
                        b.w(v, "s_pe", pe_bd[d])
                        b.w(v, f"s_xf2r{d % 2}", 16 * (d // 2 + 1))
                        ps_ = psA if d % 2 == 0 else psB
                        b.inc("s_dv", v.scalar_tensor_tensor(r3[:, d * 1024:(d + 1) * 1024],
                                                             ps_[:], 0.5, x2fb[d % 2][:],
                                                             ALU.mult, ALU.add))
                        assert b.cnt["s_dv"] == dv_r3[d]
                    b.w(v, "s_pe", pe_st2[31])
                    b.inc("s_dv", v.tensor_scalar(ln2[0:1, 0:1024], psE[0:1, 0:1024],
                                                  1.0 / 4096.0, None, ALU.mult))
                    b.w(v, "s_dv")
                    b.inc("s_dv", v.tensor_tensor(ln2[0:1, 1024:2048], ln2[0:1, 0:1024],
                                                  ln2[0:1, 0:1024], ALU.mult))
                    b.w(v, "s_dv")
                    b.inc("s_dv", v.scalar_tensor_tensor(ln2[0:1, 2048:3072], psE[0:1, 1024:2048],
                                                         1.0 / 4096.0, ln2[0:1, 1024:2048],
                                                         ALU.mult, ALU.subtract))
                    b.w(v, "s_ac", AC_LN2b)
                    b.inc("s_dv", v.tensor_tensor(ln2[0:1, 3072:4096], ln2[0:1, 0:1024],
                                                  ln2[0:1, 2048:3072], ALU.mult))
                    b.w(v, "s_dv")
                    b.inc("s_dv", v.tensor_scalar(ln2[0:1, 3072:4096], ln2[0:1, 3072:4096],
                                                  -1.0, None, ALU.mult))
                    assert b.cnt["s_dv"] == DV_LN2

        return nc, T


def host_prep(inp):
    hs = inp["L0_attn_hidden_states"]
    W1T = np.ascontiguousarray(inp["L0_attn_in_proj_w"].T)
    pr = {}
    pr["xT"] = [np.ascontiguousarray(hs[k // 4, (k % 4) * TOK:(k % 4 + 1) * TOK].T) for k in range(8)]
    Wg, Wx, Wd = W1T[:, 0:INTER], W1T[:, INTER:INTER + CONV_DIM], W1T[:, INTER + CONV_DIM:]
    pr["W_xbc"] = np.ascontiguousarray(Wx.reshape(32, 128, 80, 128).transpose(2, 0, 1, 3))
    pr["W_gate"] = np.ascontiguousarray(Wg.reshape(32, 128, 64, 128).transpose(2, 0, 1, 3))
    pr["W_dt"] = np.ascontiguousarray(Wd.reshape(32, 128, 128))
    pr["W2"] = np.ascontiguousarray(
        inp["L0_attn_out_proj_w"].T.reshape(64, 128, 32, 128).transpose(2, 0, 1, 3))
    pr["convw"] = np.ascontiguousarray(
        inp["L0_attn_conv_w"][:, 0, :].reshape(80, 128, 4).transpose(1, 0, 2).reshape(128, 320))
    pr["convb"] = np.ascontiguousarray(inp["L0_attn_conv_b"].reshape(80, 128).T)
    pr["dt_bias"] = np.ascontiguousarray(inp["L0_attn_dt_bias"].reshape(128, 1))
    pr["D_row"] = np.ascontiguousarray(inp["L0_attn_D"].reshape(1, 128))
    pr["norm_w"] = np.ascontiguousarray(inp["L0_attn_norm_w"].reshape(64, 128).T)
    pr["router_wT"] = np.ascontiguousarray(inp["L0_ffn_router_w"].T.reshape(32, 128, 8))
    pr["router_b"] = np.ascontiguousarray(inp["L0_ffn_router_b"].reshape(8, 1))
    pr["AT"] = np.ascontiguousarray(inp["L0_ffn_A"].reshape(128, 4096).T.reshape(32, 128, 128))
    pr["BT"] = np.ascontiguousarray(inp["L0_ffn_Bm"].transpose(0, 2, 1).reshape(128, 4096))
    pr["mask8e"] = [np.tile(np.array([1.0 if (j // 4 == k // 4 and j < k) else 0.0
                                      for j in range(8)], np.float32), (128, 1)) for k in range(8)]
    pr["hmaske"] = [np.tile(np.array([1.0 if (j == k - 1 and k % 4 != 0) else 0.0
                                      for j in range(8)], np.float32), (128, 1)) for k in range(8)]
    fs = []
    for k in range(8):
        f = np.zeros((8, 8), np.float32)
        for j in range(8):
            for m in range(8):
                if (j // 4 == k // 4 and j < k) and (m // 4 == k // 4 and m < k) and m > j:
                    f[j, m] = 1.0
        fs.append(np.tile(f.reshape(1, 64), (128, 1)))
    pr["fsele"] = fs
    return pr


def in_maps(pr):
    common = {k: pr[k] for k in ["W_xbc", "W_gate", "W_dt", "W2", "convw", "convb",
                                 "dt_bias", "D_row", "norm_w", "router_wT", "router_b",
                                 "AT", "BT"]}
    return [dict(common, xT=pr["xT"][k], mask8e=pr["mask8e"][k],
                 hmaske=pr["hmaske"][k], fsele=pr["fsele"][k]) for k in range(8)]


def kernel(**inputs):
    pr = host_prep(inputs)
    nc, T = build(PH=99)
    res = run_bass_kernel_spmd(nc, in_maps(pr), core_ids=list(range(8)))
    out = np.stack([np.concatenate([res.results[b * 4 + p]["out"] for p in range(4)], 0)
                    for b in range(2)])
    return out.astype(np.float32, copy=False)
